# version 01
# baseline (speedup 1.0000x reference)
import numpy as np
import concourse.bass as bass
import concourse.mybir as mybir
from concourse.bass_utils import run_bass_kernel_spmd

F32 = mybir.dt.float32
BF16 = mybir.dt.bfloat16
I32 = mybir.dt.int32
U16 = mybir.dt.uint16
U8 = mybir.dt.uint8
AL = mybir.AluOpType
AF = mybir.ActivationFunctionType
AX = mybir.AxisListType.X

B_CORE = 4
S = 2048
DM = 1024
NST = 16
DCH = 8
I3 = 3072
DI = 2048
NSTATE = 16
RK = 64
TC = 512
NCH = S // TC

DEV_STOP = [None]


class Plan:
    ENGS = ("sync", "gpsimd", "vector", "scalar", "tensor")

    def __init__(self):
        self.prog = {e: [] for e in self.ENGS}
        self.cnt = {}

    def inc(self, sem, k=1):
        self.cnt[sem] = self.cnt.get(sem, 0) + k
        return self.cnt[sem]

    def val(self, sem):
        return self.cnt.get(sem, 0)

    def emit(self, eng, fn):
        import traceback
        st = traceback.extract_stack(limit=6)
        fn._plan_site = [f"{f.filename.split('/')[-1]}:{f.lineno}" for f in st[:-2]]
        self.prog[eng].append(fn)

    def wait(self, eng, sem, v=None):
        if v is None:
            v = self.val(sem)
        if v <= 0:
            return
        self.emit(eng, lambda E, SM, sem=sem, v=v: E.wait_ge(SM[sem], v))


def kernel(**inputs):
    x_full = np.ascontiguousarray(inputs["L0_attn_x"], dtype=np.float32)
    shared = dict(
        wq=np.ascontiguousarray(inputs["L0_attn_Wq"].reshape(DM, DM), dtype=np.float32),
        wk=np.ascontiguousarray(inputs["L0_attn_Wk"].reshape(DM, DM), dtype=np.float32),
        wv=np.ascontiguousarray(inputs["L0_attn_Wv"].reshape(DM, DM), dtype=np.float32),
        wo=np.ascontiguousarray(inputs["L0_attn_Wo"].reshape(DM, DM), dtype=np.float32),
        bq=np.ascontiguousarray(inputs["L0_attn_bq"].reshape(DM), dtype=np.float32),
        bk=np.ascontiguousarray(inputs["L0_attn_bk"].reshape(DM), dtype=np.float32),
        bv=np.ascontiguousarray(inputs["L0_attn_bv"].reshape(DM), dtype=np.float32),
        bo=np.ascontiguousarray(inputs["L0_attn_bo"], dtype=np.float32),
        wgate=np.ascontiguousarray(inputs["L0_ffn_w_gate"], dtype=np.float32),
        wup=np.ascontiguousarray(inputs["L0_ffn_w_up"], dtype=np.float32),
        wdown=np.ascontiguousarray(inputs["L0_ffn_w_down"], dtype=np.float32),
        ln_g=np.ascontiguousarray(inputs["L1_attn_ln_g"], dtype=np.float32),
        ln_b=np.ascontiguousarray(inputs["L1_attn_ln_b"], dtype=np.float32),
        w_in=np.ascontiguousarray(inputs["L1_attn_W_in"], dtype=np.float32),
        conv_w=np.ascontiguousarray(inputs["L1_attn_conv_w"], dtype=np.float32),
        conv_b=np.ascontiguousarray(inputs["L1_attn_conv_b"], dtype=np.float32),
        w_x=np.ascontiguousarray(inputs["L1_attn_W_x"], dtype=np.float32),
        w_dt=np.ascontiguousarray(inputs["L1_attn_W_dt"], dtype=np.float32),
        b_dt=np.ascontiguousarray(inputs["L1_attn_b_dt"], dtype=np.float32),
        a_log=np.ascontiguousarray(inputs["L1_attn_A_log"], dtype=np.float32),
        d_vec=np.ascontiguousarray(inputs["L1_attn_D"], dtype=np.float32),
        w_out=np.ascontiguousarray(inputs["L1_attn_W_out"], dtype=np.float32),
        f_w=np.ascontiguousarray(inputs["L1_ffn_weight"], dtype=np.float32),
        f_b=np.ascontiguousarray(inputs["L1_ffn_bias"], dtype=np.float32),
    )
    nc = build()
    in_maps = []
    for c in range(8):
        m = dict(shared)
        m["x"] = np.ascontiguousarray(
            x_full[c * B_CORE:(c + 1) * B_CORE].reshape(B_CORE * S, DM))
        in_maps.append(m)
    res = run_bass_kernel_spmd(nc, in_maps, core_ids=list(range(8)))
    out = np.concatenate(
        [r["out"].reshape(B_CORE, S, DM) for r in res.results], axis=0)
    return out


def build():
    nc = bass.Bass("TRN2", target_bir_lowering=False)

    def din(name, shape, dtype=F32):
        return nc.dram_tensor(name, list(shape), dtype, kind="ExternalInput").ap()

    x_h = din("x", [B_CORE * S, DM])
    wq_h = din("wq", [DM, DM]); wk_h = din("wk", [DM, DM])
    wv_h = din("wv", [DM, DM]); wo_h = din("wo", [DM, DM])
    bq_h = din("bq", [DM]); bk_h = din("bk", [DM]); bv_h = din("bv", [DM])
    bo_h = din("bo", [DM])
    wgate_h = din("wgate", [I3, DM]); wup_h = din("wup", [I3, DM])
    wdown_h = din("wdown", [DM, I3])
    lng_h = din("ln_g", [DM]); lnb_h = din("ln_b", [DM])
    win_h = din("w_in", [DM, 2 * DI])
    convw_h = din("conv_w", [DI, 4]); convb_h = din("conv_b", [DI])
    wx_h = din("w_x", [DI, 96]); wdt_h = din("w_dt", [RK, DI])
    bdt_h = din("b_dt", [DI]); alog_h = din("a_log", [DI, NSTATE])
    dvec_h = din("d_vec", [DI]); wout_h = din("w_out", [DI, DM])
    fw_h = din("f_w", [DM, DM]); fb_h = din("f_b", [DM])

    out_h = nc.dram_tensor("out", [B_CORE * S, DM], F32, kind="ExternalOutput").ap()

    def dint(name, shape, dtype=F32):
        return nc.dram_tensor(name, list(shape), dtype).ap()

    h_x1 = dint("h_x1", [B_CORE * S, DM])
    h_x2 = dint("h_x2", [B_CORE * S, DM])
    h_x3 = dint("h_x3", [B_CORE * S, DM])
    h_tbf = dint("h_tbf", [B_CORE * S, DM], BF16)
    h_upraw = dint("h_upraw", [B_CORE, I3, S])
    h_gateraw = dint("h_gateraw", [B_CORE, I3, S])
    h_cup = dint("h_cup", [B_CORE, I3, S], U16)
    h_cg = dint("h_cg", [B_CORE, I3, S], U16)
    h_cs = dint("h_cs", [B_CORE, I3, S], U16)
    h_cg2 = dint("h_cg2", [B_CORE, I3, S], U16)
    h_obf = dint("h_obf", [B_CORE, I3, S], BF16)
    h_xc = dint("h_xc", [B_CORE, DI, S])
    h_zs = dint("h_zs", [B_CORE, DI, S])
    h_u1 = dint("h_u1", [B_CORE * S, DM])
    h_yg = dint("h_yg", [B_CORE, DI, S], BF16)
    h_bc = dint("h_bc", [B_CORE, 32, S])
    h_zrec = dint("h_zrec", [B_CORE, 16, S])
    hw_q = dint("hw_q", [DM, DM], BF16); hw_k = dint("hw_k", [DM, DM], BF16)
    hw_v = dint("hw_v", [DM, DM], BF16); hw_o = dint("hw_o", [DM, DM], BF16)
    hw_in = dint("hw_in", [DM, 2 * DI], BF16)
    hw_x = dint("hw_x", [DI, 96], BF16); hw_dt = dint("hw_dt", [RK, DI], BF16)
    hw_out = dint("hw_out", [DI, DM], BF16)
    hw_upq = dint("hw_upq", [I3, DM], BF16)
    hw_gateq = dint("hw_gateq", [I3, DM], BF16)
    hw_downq = dint("hw_downq", [DM, I3], BF16)
    hw_fq = dint("hw_fq", [DM, DM], BF16)
    hw_upT = dint("hw_upT", [DM, I3], BF16)
    hw_gateT = dint("hw_gateT", [DM, I3], BF16)
    hw_downT = dint("hw_downT", [I3, DM], BF16)
    hw_fT = dint("hw_fT", [DM, DM], BF16)
    cc_in = nc.dram_tensor("cc_in", [2, 1], F32)
    cc_out = nc.dram_tensor("cc_out", [2, 1], F32)
    h_pmax = dint("h_pmax", [2, 128])

    ctxs = []

    def sbuf(name, p, free, dtype=F32):
        cm = nc.sbuf_tensor(name, [p, free], dtype)
        t = cm.__enter__()[:, :]
        ctxs.append(cm)
        return t

    c_bq = sbuf("c_bq", 128, DCH)
    c_bk = sbuf("c_bk", 128, DM)
    c_bv = sbuf("c_bv", 128, DM)
    c_bo = sbuf("c_bo", 128, DM)
    c_lng = sbuf("c_lng", 128, DM)
    c_lnb = sbuf("c_lnb", 128, DM)
    c_fb = sbuf("c_fb", 128, DM)
    c_A = sbuf("c_A", 128, 16 * NSTATE)
    c_D = sbuf("c_D", 128, 16)
    c_cw = sbuf("c_cw", 128, 16 * 4)
    c_cb = sbuf("c_cb", 128, 16)
    c_bdt = sbuf("c_bdt", 128, 16)
    c_sc = sbuf("c_sc", 128, 32)
    c_ones = sbuf("c_ones", 128, 8, BF16)

    st_m = {k: sbuf(f"st_m_{k}", 128, 16 * B_CORE) for k in ("ln3", "ln5", "ln6")}
    st_r = {k: sbuf(f"st_r_{k}", 128, 16 * B_CORE) for k in ("ln3", "ln5", "ln6")}
    st_tmp = sbuf("st_tmp", 128, 192)
    a_den16 = sbuf("a_den16", 16, S)
    m_dbc = sbuf("m_dbc", 96, S)
    m_dtpre = sbuf("m_dtpre", 64, S, BF16)
    s_wdt = sbuf("s_wdt", 64, DI, BF16)

    ARENA_BYTES = 152 * 1024
    arena_cm = nc.sbuf_tensor("arena", [128, ARENA_BYTES], U8)
    arena = arena_cm.__enter__()[:, :]
    ctxs.append(arena_cm)

    class Carver:
        def __init__(self):
            self.off = 0

        def take(self, free_elems, dtype=F32):
            sz = {F32: 4, BF16: 2, U16: 2, I32: 4, U8: 1}[dtype]
            off = (self.off + 63) // 64 * 64
            nb = free_elems * sz
            assert off + nb <= ARENA_BYTES, (off, nb, "arena overflow")
            self.off = off + nb
            sl = arena[:, off:off + nb]
            if dtype != U8:
                sl = sl.bitcast(dtype)
            return sl

    ps_cm1 = nc.psum_tensor("psA", [128, 2048], F32)
    psA = ps_cm1.__enter__()[:, :]; ctxs.append(ps_cm1)
    ps_cm2 = nc.psum_tensor("psB", [128, 2048], F32)
    psB = ps_cm2.__enter__()[:, :]; ctxs.append(ps_cm2)

    sem_cms = {}
    for snm in ("dl", "dg", "v", "s", "t", "cc"):
        cm = nc.semaphore(snm)
        sem_cms[snm] = cm.__enter__()
        ctxs.append(cm)

    P = Plan()

    def dma(q, sem, out_ap, in_ap, **kw):
        P.inc(sem, 16)
        P.emit(q, lambda E, SM, o=out_ap, i=in_ap, kw=dict(kw), s=sem:
               E.dma_start(out=o, in_=i, **kw).then_inc(SM[s], 16))

    def _small_out(args):
        try:
            return args[0].free_size() < 64
        except Exception:
            return True

    VOP_ALWAYS_DRAIN = [True]

    def vop(meth, *args, **kw):
        P.inc("v", 1)
        if VOP_ALWAYS_DRAIN[0] or _small_out(args):
            def f(E, SM, m=meth, a=args, k=dict(kw)):
                getattr(E, m)(*a, **k)
                E.drain().then_inc(SM["v"], 1)
        else:
            def f(E, SM, m=meth, a=args, k=dict(kw)):
                getattr(E, m)(*a, **k).then_inc(SM["v"], 1)
        P.emit("vector", f)

    def sop(meth, *args, **kw):
        P.inc("s", 1)
        if _small_out(args) or "accum_out" in kw:
            def f(E, SM, m=meth, a=args, k=dict(kw)):
                getattr(E, m)(*a, **k)
                E.drain().then_inc(SM["s"], 1)
        else:
            def f(E, SM, m=meth, a=args, k=dict(kw)):
                getattr(E, m)(*a, **k).then_inc(SM["s"], 1)
        P.emit("scalar", f)

    def top(out_ap, lhsT, rhs, start, stop):
        P.inc("t", 1)
        P.emit("tensor", lambda E, SM, o=out_ap, l=lhsT, r=rhs, st=start, sp=stop:
               E.matmul(o, l, r, start=st, stop=sp).then_inc(SM["t"], 1))

    def allreduce2(n):
        P.inc("cc", 1)
        P.emit("gpsimd", lambda E, SM, n=n:
               E.collective_compute("AllReduce", AL.max,
                                    replica_groups=[list(range(8))],
                                    ins=[cc_in.ap()[0:n, :].opt()],
                                    outs=[cc_out.ap()[0:n, :].opt()]).then_inc(SM["cc"], 1))

    def barrier_compute():
        for e in P.ENGS:
            for sm in ("dl", "dg", "v", "s", "t", "cc"):
                P.wait(e, sm)

    def rowstats(tile_f32, ncols, mcol, rcol, eps, sq_scr):
        s_sum = st_tmp[:, 0:1]; s_sq = st_tmp[:, 1:2]
        s_var = st_tmp[:, 2:3]; s_g = st_tmp[:, 3:4]
        s_r0 = st_tmp[:, 4:5]; s_t = st_tmp[:, 5:6]
        P.wait("scalar", "v")
        P.wait("scalar", "dl")
        vop("reduce_sum", s_sum, tile_f32, AX)
        sop("activation", sq_scr, tile_f32, AF.Square, accum_out=s_sq)
        vop("tensor_single_scalar", mcol, s_sum, 1.0 / ncols, AL.mult)
        P.wait("vector", "s")
        vop("tensor_single_scalar", s_var, s_sq, 1.0 / ncols, AL.mult)
        vop("tensor_tensor", s_t, mcol, mcol, AL.mult)
        vop("tensor_tensor", s_var, s_var, s_t, AL.subtract)
        vop("tensor_single_scalar", s_g, s_var, eps, AL.add)
        P.wait("scalar", "v")
        sop("activation", s_t, s_g, AF.Sqrt)
        P.wait("vector", "s")
        vop("reciprocal", s_r0, s_t)
        vop("tensor_tensor", s_t, s_r0, s_r0, AL.mult)
        vop("tensor_tensor", s_t, s_t, s_g, AL.mult)
        vop("tensor_single_scalar", s_t, s_t, -0.5, AL.mult)
        vop("tensor_single_scalar", s_t, s_t, 1.5, AL.add)
        vop("tensor_tensor", rcol, s_r0, s_t, AL.mult)

    def negmr(mcol, rcol, outcol):
        vop("tensor_tensor", outcol, mcol, rcol, AL.mult)
        vop("tensor_single_scalar", outcol, outcol, -1.0, AL.mult)

    def floor_fixup(z_f32, ci32, g_scr, out_f32):
        vop("tensor_copy", ci32, z_f32)
        vop("tensor_tensor", g_scr, ci32, z_f32, AL.is_gt)
        vop("tensor_tensor", out_f32, ci32, g_scr, AL.subtract)

    for src, dst in ((wq_h, hw_q), (wk_h, hw_k), (wv_h, hw_v), (wo_h, hw_o),
                     (win_h, hw_in), (wx_h, hw_x), (wdt_h, hw_dt),
                     (wout_h, hw_out)):
        dma("gpsimd", "dg", dst, src)

    dma("sync", "dl", c_bq, bq_h.rearrange("(g p) -> p g", p=128), allow_slow_non_contiguous=True)
    dma("sync", "dl", c_bk, bk_h.unsqueeze(0).broadcast_to([128, DM]))
    dma("sync", "dl", c_bv, bv_h.unsqueeze(0).broadcast_to([128, DM]))
    dma("sync", "dl", c_bo, bo_h.unsqueeze(0).broadcast_to([128, DM]))
    dma("sync", "dl", c_lng, lng_h.unsqueeze(0).broadcast_to([128, DM]))
    dma("sync", "dl", c_lnb, lnb_h.unsqueeze(0).broadcast_to([128, DM]))
    dma("sync", "dl", c_fb, fb_h.unsqueeze(0).broadcast_to([128, DM]))
    dma("sync", "dl", c_A.rearrange("p (g n) -> p g n", n=NSTATE),
        alog_h.rearrange("(g p) n -> p g n", p=128), allow_slow_non_contiguous=True)
    dma("sync", "dl", c_D, dvec_h.rearrange("(g p) -> p g", p=128), allow_slow_non_contiguous=True)
    dma("sync", "dl", c_cw.rearrange("p (g k) -> p g k", k=4),
        convw_h.rearrange("(g p) k -> p g k", p=128), allow_slow_non_contiguous=True)
    dma("sync", "dl", c_cb, convb_h.rearrange("(g p) -> p g", p=128), allow_slow_non_contiguous=True)
    dma("sync", "dl", c_bdt, bdt_h.rearrange("(g p) -> p g", p=128), allow_slow_non_contiguous=True)
    P.wait("sync", "dg")
    dma("sync", "dl", s_wdt, hw_dt)
    P.wait("vector", "dl")
    P.wait("scalar", "dl")
    P.wait("scalar", "dg")
    sop("activation", c_A, c_A, AF.Exp)
    P.wait("vector", "s")
    vop("tensor_single_scalar", c_A, c_A, -1.0, AL.mult)
    vop("memset", c_ones, 1.0)
    vop("memset", c_sc[:, 23:24], 32768.5)
    vop("memset", c_sc[:, 24:25], 0.5)

    cv = Carver()
    wt = cv.take(1024)
    w_bmax = cv.take(32); w_inv = cv.take(32); w_s = cv.take(32)
    wq_t = cv.take(1024); wz = cv.take(1024); wg = cv.take(1024)
    wi32 = cv.take(1024, I32)
    wbf = cv.take(1024, BF16)
    wtr = cv.take(I3, BF16)

    def lpbq_tile(src_rows_ap, dst_rows_ap):
        P.wait("sync", "v")
        dma("sync", "dl", wt, src_rows_ap)
        P.wait("vector", "dl")
        w3 = wt.rearrange("p (g k) -> p g k", k=32)
        vop("tensor_reduce", w_bmax, w3, AX, AL.max, apply_absolute_value=True)
        vop("tensor_single_scalar", w_s, w_bmax, 1.0 / 7.0, AL.mult)
        vop("tensor_single_scalar", w_s, w_s, 1e-12, AL.max)
        vop("reciprocal", w_inv, w_s)
        q3 = wq_t.rearrange("p (g k) -> p g k", k=32)
        vop("tensor_tensor", q3, w3,
            w_inv.unsqueeze(2).broadcast_to([128, 32, 32]), AL.mult)
        vop("tensor_single_scalar", wz, wq_t, 16.5, AL.add)
        floor_fixup(wz, wi32, wg, wz)
        vop("tensor_single_scalar", wz, wz, -16.0, AL.add)
        z3 = wz.rearrange("p (g k) -> p g k", k=32)
        b3 = wbf.rearrange("p (g k) -> p g k", k=32)
        vop("tensor_tensor", b3, z3,
            w_s.unsqueeze(2).broadcast_to([128, 32, 32]), AL.mult)
        P.wait("sync", "v")
        dma("sync", "dl", dst_rows_ap, wbf)

    for (srcw, dstw) in ((wup_h, hw_upq), (wgate_h, hw_gateq)):
        for rc in range(I3 // 128):
            lpbq_tile(srcw[rc * 128:(rc + 1) * 128, :],
                      dstw[rc * 128:(rc + 1) * 128, :])
    for rc in range(DM // 128):
        for cc2 in range(3):
            lpbq_tile(wdown_h[rc * 128:(rc + 1) * 128, cc2 * 1024:(cc2 + 1) * 1024],
                      hw_downq[rc * 128:(rc + 1) * 128, cc2 * 1024:(cc2 + 1) * 1024])

    def fp6_tile(src_rows_ap, dst_rows_ap):
        ncols, gblk = DM, DM // 32
        P.wait("sync", "v")
        dma("sync", "dl", wt, src_rows_ap)
        P.wait("vector", "dl")
        w3 = wt.rearrange("p (g k) -> p g k", k=32)
        vop("tensor_reduce", w_bmax, w3, AX, AL.max, apply_absolute_value=True)
        vop("tensor_single_scalar", w_s, w_bmax, 1.0 / 7.5, AL.mult)
        vop("tensor_single_scalar", w_s, w_s, 1e-30, AL.max)
        vop("reciprocal", w_inv, w_s)
        q3 = wq_t.rearrange("p (g k) -> p g k", k=32)
        vop("tensor_tensor", q3, w3,
            w_inv.unsqueeze(2).broadcast_to([128, gblk, 32]), AL.mult)
        P.wait("scalar", "v")
        sop("activation", wg, wq_t, AF.Abs)
        P.wait("vector", "s")
        t1 = wz
        vop("tensor_single_scalar", t1, wg, 2.0, AL.is_ge)
        vop("tensor_single_scalar", t1, t1, -4.0, AL.mult)
        vop("tensor_single_scalar", t1, t1, 8.0, AL.add)
        vop("tensor_single_scalar", wg, wg, 4.0, AL.is_ge)
        vop("tensor_single_scalar", wg, wg, -2.0, AL.mult)
        vop("tensor_tensor", t1, t1, wg, AL.add)
        vop("tensor_tensor", wq_t, wq_t, t1, AL.mult)
        vop("tensor_single_scalar", wq_t, wq_t, -1.0, AL.mult)
        vop("tensor_single_scalar", wq_t, wq_t, 32768.5, AL.add)
        floor_fixup(wq_t, wi32, wg, wq_t)
        vop("tensor_single_scalar", wq_t, wq_t, -32768.0, AL.add)
        vop("tensor_single_scalar", wq_t, wq_t, -1.0, AL.mult)
        vop("reciprocal", t1, t1)
        vop("tensor_tensor", wq_t, wq_t, t1, AL.mult)
        b3 = wbf.rearrange("p (g k) -> p g k", k=32)
        vop("tensor_tensor", b3, wq_t.rearrange("p (g k) -> p g k", k=32),
            w_s.unsqueeze(2).broadcast_to([128, gblk, 32]), AL.mult)
        P.wait("sync", "v")
        dma("sync", "dl", dst_rows_ap, wbf)

    for rc in range(DM // 128):
        fp6_tile(fw_h[rc * 128:(rc + 1) * 128, :], hw_fq[rc * 128:(rc + 1) * 128, :])

    def transpose_weight(src_bf, dst_bf, nrows, ncols):
        for ic in range(ncols // 128):
            P.wait("sync", "dl")
            dma("sync", "dl", wtr[:, 0:nrows],
                src_bf[:, ic * 128:(ic + 1) * 128], transpose=True)
            P.wait("sync", "dl")
            dma("sync", "dl", dst_bf[ic * 128:(ic + 1) * 128, :], wtr[:, 0:nrows])

    transpose_weight(hw_upq, hw_upT, I3, DM)
    transpose_weight(hw_gateq, hw_gateT, I3, DM)
    transpose_weight(hw_downq, hw_downT, DM, I3)
    transpose_weight(hw_fq, hw_fT, DM, DM)

    barrier_compute()

    cv = Carver()
    a_xn = cv.take(DM)
    a_xnbf = cv.take(DM, BF16)
    a_sq = cv.take(DM)
    a_xnT = cv.take(DCH * S, BF16)
    a_yz = a_xnT
    a_wtile = cv.take(DCH * DM, BF16)
    a_qf = cv.take(DCH * S, BF16)
    a_wtile2 = cv.take(DCH * DM, BF16)
    a_kft = cv.take(DM, BF16)
    a_vvt = cv.take(DM, BF16)
    a_ev1 = cv.take(S)
    a_ev2 = cv.take(S)
    a_kv = cv.take(16 * 128, BF16)
    a_zrep = cv.take(S)
    a_ao = cv.take(DM)
    a_xrow = cv.take(DM)
    a_x1 = cv.take(DM)

    def ln_rows_to_bf(src_row_ap, mcol, rcol, dst_bf, eps):
        rowstats(src_row_ap, DM, mcol, rcol, eps, a_sq)
        bias = st_tmp[:, 8:9]
        negmr(mcol, rcol, bias)
        P.wait("scalar", "v")
        sop("activation", dst_bf, src_row_ap, AF.Identity, bias=bias, scale=rcol)

    def ph0():
        for b in range(B_CORE):
            rows0 = b * S
            for st in range(NST):
                r0 = rows0 + st * 128
                P.wait("sync", "s"); P.wait("sync", "v")
                dma("sync", "dl", a_xn, x_h[r0:r0 + 128, :])
                P.wait("vector", "dl")
                mcol = st_m["ln5"][:, b * 16 + st:b * 16 + st + 1]
                rcol = st_r["ln5"][:, b * 16 + st:b * 16 + st + 1]
                ln_rows_to_bf(a_xn, mcol, rcol, a_xnbf, 1e-5)
                P.wait("sync", "s")
                dma("sync", "dl", h_tbf[r0:r0 + 128, :], a_xnbf)
            if DEV_STOP[0] == "xn" and b == 0:
                P.wait("sync", "v")
                dma("sync", "dl", out_h[4096:4096 + 128, 0:16], st_m["ln5"][:, 0:16])
                dma("sync", "dl", out_h[4096:4096 + 128, 16:32], st_r["ln5"][:, 0:16])
                dma("sync", "dl", out_h[4096:4096 + 128, 32:40], st_tmp[:, 0:8])
                for st in range(NST):
                    r0 = st * 128
                    P.wait("scalar", "dl")
                    dma("sync", "dl", a_xn.bitcast(BF16)[:, 0:DM], h_tbf[r0:r0 + 128, :])
                    P.wait("scalar", "dl")
                    sop("activation", a_sq, a_xn.bitcast(BF16)[:, 0:DM], AF.Identity)
                    P.wait("sync", "s")
                    dma("sync", "dl", out_h[r0:r0 + 128, :], a_sq)
            P.wait("sync", "t")
            for kc in range(DCH):
                dma("sync", "dl", a_xnT[:, kc * S:(kc + 1) * S],
                    h_tbf[rows0:rows0 + S, kc * 128:(kc + 1) * 128], transpose=True)
            P.wait("sync", "t")
            dma("sync", "dl", a_wtile.rearrange("p (kc m) -> p kc m", kc=hw_q.shape[0] // 128),
            hw_q.rearrange("(kc p) m -> p kc m", p=128))
            P.wait("tensor", "dl")
            for mc in range(DCH):
                P.wait("tensor", "s"); P.wait("tensor", "v")
                for kc in range(DCH):
                    for nb in range(4):
                        top(psA[:, nb * 512:(nb + 1) * 512],
                            a_wtile[:, kc * DM + mc * 128:kc * DM + (mc + 1) * 128],
                            a_xnT[:, kc * S + nb * 512:kc * S + (nb + 1) * 512],
                            start=(kc == 0), stop=(kc == DCH - 1))
                P.wait("scalar", "t"); P.wait("scalar", "v")
                sop("activation", a_ev1, psA, AF.Identity, bias=c_bq[:, mc:mc + 1])
                P.wait("vector", "s")
                vop("tensor_single_scalar", a_ev2, a_ev1, 0.0, AL.min)
                P.wait("scalar", "v")
                sop("activation", a_ev2, a_ev2, AF.Exp)
                P.wait("vector", "s")
                vop("scalar_tensor_tensor", a_ev1, a_ev1, 0.0, a_ev2, AL.max, AL.add)
                P.wait("scalar", "v")
                sop("activation", a_qf[:, mc * S:(mc + 1) * S], a_ev1, AF.Identity)
                if DEV_STOP[0] == "q" and b == 0:
                    P.wait("sync", "s"); P.wait("sync", "v")
                    dma("sync", "dl", out_h[mc * 128:(mc + 1) * 128, :],
                        a_ev1[:, 0:1024])
            if DEV_STOP[0] == "qf" and b == 0:
                for mc in range(DCH):
                    P.wait("scalar", "v"); P.wait("scalar", "t")
                    P.wait("scalar", "dl")
                    sop("activation", a_ev1[:, 0:1024], a_qf[:, mc * S:mc * S + 1024],
                        AF.Identity)
                    P.wait("sync", "s")
                    dma("sync", "dl", out_h[mc * 128:(mc + 1) * 128, :], a_ev1[:, 0:1024])
            P.wait("sync", "t")
            dma("sync", "dl", a_wtile.rearrange("p (kc m) -> p kc m", kc=DCH),
                hw_k.rearrange("(kc p) m -> p kc m", p=128))
            dma("sync", "dl", a_wtile2.rearrange("p (kc m) -> p kc m", kc=DCH),
                hw_v.rearrange("(kc p) m -> p kc m", p=128))
            for mc in range(NST):
                if mc == 0:
                    P.wait("tensor", "dl")
                P.wait("tensor", "s"); P.wait("tensor", "v")
                for kc in range(DCH):
                    for nb in range(2):
                        top(psA[:, nb * 512:(nb + 1) * 512],
                            a_xnT[:, kc * S + mc * 128:kc * S + (mc + 1) * 128],
                            a_wtile[:, kc * DM + nb * 512:kc * DM + (nb + 1) * 512],
                            start=(kc == 0), stop=(kc == DCH - 1))
                P.wait("vector", "t"); P.wait("vector", "s")
                vop("tensor_tensor", a_ev1[:, 0:1024], psA[:, 0:1024], c_bk, AL.add)
                vop("tensor_single_scalar", a_ev2[:, 0:1024],
                    a_ev1[:, 0:1024], 0.0, AL.min)
                P.wait("scalar", "v")
                sop("activation", a_ev2[:, 0:1024], a_ev2[:, 0:1024], AF.Exp)
                P.wait("vector", "s")
                vop("scalar_tensor_tensor", a_ev1[:, 0:1024],
                    a_ev1[:, 0:1024], 0.0, a_ev2[:, 0:1024], AL.max, AL.add)
                P.wait("scalar", "v")
                sop("activation", a_kft, a_ev1[:, 0:1024], AF.Identity)
                P.wait("tensor", "v")
                for kc in range(DCH):
                    for nb in range(2):
                        top(psA[:, 1024 + nb * 512:1024 + (nb + 1) * 512],
                            a_xnT[:, kc * S + mc * 128:kc * S + (mc + 1) * 128],
                            a_wtile2[:, kc * DM + nb * 512:kc * DM + (nb + 1) * 512],
                            start=(kc == 0), stop=(kc == DCH - 1))
                P.wait("vector", "t")
                vop("tensor_tensor", a_ev2[:, 0:1024], psA[:, 1024:2048], c_bv, AL.add)
                P.wait("scalar", "v")
                sop("activation", a_vvt, a_ev2[:, 0:1024], AF.Identity)
                P.wait("tensor", "v"); P.wait("tensor", "s")
                for h in range(16):
                    top(psB[0:64, h * 128:h * 128 + 64],
                        a_kft[:, h * 64:(h + 1) * 64],
                        a_vvt[:, h * 64:(h + 1) * 64],
                        start=(mc == 0 and h % 4 == 0), stop=(mc == NST - 1))
                    top(psB[0:64, h * 128 + 64:h * 128 + 65],
                        a_kft[:, h * 64:(h + 1) * 64],
                        c_ones[0:128, 0:1],
                        start=False, stop=(mc == NST - 1))
            P.wait("scalar", "t")
            sop("activation", a_kv[0:64, :], psB[0:64, 0:2048], AF.Identity)
            P.wait("gpsimd", "s")
            dma("gpsimd", "dg", a_kv[64:128, :], a_kv[0:64, :])
            P.wait("tensor", "dg"); P.wait("tensor", "v")
            if DEV_STOP[0] == "kv" and b == 0:
                P.wait("scalar", "dg")
                sop("activation", a_ev1[:, 0:1024], a_kv[:, 0:1024], AF.Identity)
                P.wait("sync", "s")
                dma("sync", "dl", out_h[0:128, :], a_ev1[:, 0:1024])
            if DEV_STOP[0] == "den" and b == 0:
                pass
            for h in range(16):
                kq, half = h // 2, (h % 2) * 64
                P.wait("tensor", "s")
                for nb in range(4):
                    top(psA[0:1, nb * 512:(nb + 1) * 512],
                        a_kv[half:half + 64, h * 128 + 64:h * 128 + 65],
                        a_qf[half:half + 64, kq * S + nb * 512:kq * S + (nb + 1) * 512],
                        start=True, stop=True)
                P.wait("scalar", "t")
                sop("activation", a_ev1[0:1, 0:S], psA[0:1, 0:S], AF.Identity)
                P.wait("gpsimd", "s")
                dma("gpsimd", "dg", a_den16[h:h + 1, :], a_ev1[0:1, 0:S])
            P.wait("vector", "dg")
            vop("tensor_single_scalar", a_den16, a_den16, 1e-6, AL.add)
            vop("reciprocal", a_den16, a_den16)
            P.wait("sync", "v")
            dma("sync", "dl", h_zrec[b], a_den16)
            P.wait("sync", "dl")
            if DEV_STOP[0] == "den" and b == 0:
                dma("sync", "dl", out_h[0:16, :], a_den16[:, 0:1024])
            for h in range(16):
                kq, half = h % 8 // 1, 0
                kq, half = h // 2, (h % 2) * 64
                P.wait("sync", "v")
                dma("sync", "dl", a_zrep[half:half + 64, :],
                    h_zrec[b][h:h + 1, :].broadcast_to([64, S]))
                P.wait("tensor", "v")
                for nb in range(4):
                    top(psA[half:half + 64, nb * 512:(nb + 1) * 512],
                        a_kv[half:half + 64, h * 128:h * 128 + 64],
                        a_qf[half:half + 64, kq * S + nb * 512:kq * S + (nb + 1) * 512],
                        start=True, stop=True)
                P.wait("vector", "t"); P.wait("vector", "dl")
                vop("tensor_tensor", a_yz[half:half + 64, kq * S:(kq + 1) * S],
                    psA[half:half + 64, :], a_zrep[half:half + 64, :], AL.mult)
            P.wait("sync", "t")
            dma("sync", "dl", a_wtile.rearrange("p (kc m) -> p kc m", kc=hw_o.shape[0] // 128),
            hw_o.rearrange("(kc p) m -> p kc m", p=128))
            P.wait("tensor", "dl"); P.wait("tensor", "v")
            for mc in range(NST):
                P.wait("tensor", "v")
                for kc in range(DCH):
                    for nb in range(2):
                        top(psB[:, nb * 512:(nb + 1) * 512],
                            a_yz[:, kc * S + mc * 128:kc * S + (mc + 1) * 128],
                            a_wtile[:, kc * DM + nb * 512:kc * DM + (nb + 1) * 512],
                            start=(kc == 0), stop=(kc == DCH - 1))
                r0 = rows0 + mc * 128
                P.wait("sync", "v")
                dma("sync", "dl", a_xrow, x_h[r0:r0 + 128, :])
                P.wait("vector", "t"); P.wait("vector", "dl")
                vop("tensor_tensor", a_ao, psB[:, 0:1024], c_bo, AL.add)
                mcol = st_tmp[:, 16:17]; rcol = st_tmp[:, 17:18]
                rowstats(a_ao, DM, mcol, rcol, 1e-5, a_sq)
                bias = st_tmp[:, 18:19]
                negmr(mcol, rcol, bias)
                P.wait("scalar", "v")
                sop("activation", a_sq, a_ao, AF.Identity, bias=bias, scale=rcol)
                P.wait("vector", "s")
                vop("tensor_tensor", a_x1, a_xrow, a_sq, AL.add)
                m3 = st_m["ln3"][:, b * 16 + mc:b * 16 + mc + 1]
                r3 = st_r["ln3"][:, b * 16 + mc:b * 16 + mc + 1]
                rowstats(a_x1, DM, m3, r3, 1e-5, a_sq)
                rmx = st_tmp[:, 20:21]; rmn = st_tmp[:, 21:22]
                vop("reduce_max", rmx, a_x1, AX)
                vop("tensor_reduce", rmn, a_x1, AX, AL.min)
                t1 = st_tmp[:, 22:23]; t2 = st_tmp[:, 23:24]
                vop("tensor_tensor", t1, rmx, m3, AL.subtract)
                vop("tensor_tensor", t2, m3, rmn, AL.subtract)
                vop("tensor_tensor", t1, t1, t2, AL.max)
                vop("tensor_tensor", t1, t1, r3, AL.mult)
                if b == 0 and mc == 0:
                    vop("tensor_copy", st_tmp[:, 24:25], t1)
                else:
                    vop("tensor_tensor", st_tmp[:, 24:25], st_tmp[:, 24:25], t1, AL.max)
                P.wait("sync", "v")
                dma("sync", "dl", h_x1[r0:r0 + 128, :], a_x1)

    ph0()
    barrier_compute()
    if DEV_STOP[0] in ("xn", "q", "qf", "kv", "den"):
        P.wait("sync", "dl"); P.wait("sync", "dg")
        return finish(nc, P, sem_cms, ctxs)

    def dev_copyout(src_hbm, scratch):
        for b in range(B_CORE):
            for st in range(NST):
                r0 = b * S + st * 128
                P.wait("sync", "dl")
                dma("sync", "dl", scratch, src_hbm[r0:r0 + 128, :])
                P.wait("sync", "dl")
                dma("sync", "dl", out_h[r0:r0 + 128, :], scratch)
        P.wait("sync", "dl")
        P.wait("sync", "dg")

    if DEV_STOP[0] == "x1":
        dev_copyout(h_x1, a_xn)
        return finish(nc, P, sem_cms, ctxs)

    def cross_partition_scalar(col_ap, out_col_idx, second=None, second_idx=None):
        n = 2 if second is not None else 1
        P.wait("gpsimd", "v")
        dma("gpsimd", "dg", h_pmax[0:1, :], col_ap)
        if second is not None:
            dma("gpsimd", "dg", h_pmax[1:2, :], second)
        P.wait("gpsimd", "dg")
        dma("gpsimd", "dg", st_tmp[0:n, 32:32 + 128], h_pmax[0:n, :])
        P.wait("vector", "dg")
        vop("reduce_max", st_tmp[0:n, 30:31], st_tmp[0:n, 32:32 + 128], AX)
        P.wait("gpsimd", "v")
        dma("gpsimd", "dg", cc_in.ap()[0:n, :], st_tmp[0:n, 30:31])
        P.wait("gpsimd", "dg")
        allreduce2(n)
        P.wait("gpsimd", "cc")
        dma("gpsimd", "dg", c_sc[:, out_col_idx:out_col_idx + 1],
            cc_out.ap()[0:1, :].broadcast_to([128, 1]))
        if second is not None:
            dma("gpsimd", "dg", c_sc[:, second_idx:second_idx + 1],
                cc_out.ap()[1:2, :].broadcast_to([128, 1]))
        P.wait("vector", "dg")

    def mk_scale(g_i, s_i, i_i, b_i):
        g = c_sc[:, g_i:g_i + 1]; s = c_sc[:, s_i:s_i + 1]
        iv = c_sc[:, i_i:i_i + 1]; bb = c_sc[:, b_i:b_i + 1]
        vop("tensor_single_scalar", s, g, 1.0 / 32767.0, AL.mult)
        vop("tensor_single_scalar", s, s, 1e-12, AL.max)
        vop("reciprocal", iv, s)
        vop("tensor_single_scalar", bb, s, -32768.0, AL.mult)

    (SC_G1, SC_S1, SC_I1, SC_B1) = (0, 1, 2, 3)
    (SC_G2, SC_S2, SC_I2, SC_B2) = (4, 5, 6, 7)
    (SC_G3, SC_S3, SC_I3, SC_B3) = (8, 9, 10, 11)
    (SC_G4, SC_S4, SC_I4, SC_B4) = (12, 13, 14, 15)
    (SC_G5, SC_S5, SC_I5, SC_B5) = (16, 17, 18, 19)
    SC_S3H, SC_B3H, SC_I5H = 20, 21, 22

    cross_partition_scalar(st_tmp[:, 24:25], SC_G1)
    mk_scale(SC_G1, SC_S1, SC_I1, SC_B1)
    barrier_compute()

    cv = Carver()
    b_row = cv.take(DM)
    b_z = cv.take(DM)
    b_i32 = cv.take(DM, I32)
    b_g = cv.take(DM)
    b_qbf = cv.take(DM, BF16)
    b_xqT = cv.take(DCH * S, BF16)
    b_wT = cv.take(DCH * I3, BF16)
    b_stage = cv.take(2 * S)
    b_amax = cv.take(4)

    def ph1():
        for b in range(B_CORE):
            rows0 = b * S
            for st in range(NST):
                r0 = rows0 + st * 128
                P.wait("sync", "s"); P.wait("sync", "v")
                dma("sync", "dl", b_row, h_x1[r0:r0 + 128, :])
                m3 = st_m["ln3"][:, b * 16 + st:b * 16 + st + 1]
                r3 = st_r["ln3"][:, b * 16 + st:b * 16 + st + 1]
                sc = st_tmp[:, 26:27]; bi = st_tmp[:, 27:28]
                P.wait("vector", "dl"); P.wait("vector", "s")
                vop("tensor_tensor", sc, r3, c_sc[:, SC_I1:SC_I1 + 1], AL.mult)
                vop("tensor_tensor", bi, m3, sc, AL.mult)
                vop("tensor_single_scalar", bi, bi, -1.0, AL.mult)
                vop("tensor_single_scalar", bi, bi, 32768.5, AL.add)
                P.wait("scalar", "v")
                sop("activation", b_z, b_row, AF.Identity, bias=bi, scale=sc)
                P.wait("vector", "s")
                floor_fixup(b_z, b_i32, b_g, b_z)
                P.wait("scalar", "v")
                sop("activation", b_qbf, b_z, AF.Identity,
                    bias=c_sc[:, SC_B1:SC_B1 + 1], scale=c_sc[:, SC_S1:SC_S1 + 1])
                P.wait("sync", "s")
                dma("sync", "dl", h_tbf[r0:r0 + 128, :], b_qbf)
            P.wait("sync", "t")
            for kc in range(DCH):
                dma("sync", "dl", b_xqT[:, kc * S:(kc + 1) * S],
                    h_tbf[rows0:rows0 + S, kc * 128:(kc + 1) * 128], transpose=True)
            for wi, (wTsrc, rawdst, amax_i) in enumerate(
                    ((hw_upT, h_upraw, 0), (hw_gateT, h_gateraw, 1))):
                P.wait("sync", "t")
                dma("sync", "dl", b_wT.rearrange("p (kc m) -> p kc m", kc=wTsrc.shape[0] // 128),
            wTsrc.rearrange("(kc p) m -> p kc m", p=128))
                P.wait("tensor", "dl")
                for mc in range(24):
                    ps = (psA if mc % 2 == 0 else psB)
                    P.wait("tensor", "s")
                    for kc in range(DCH):
                        for nb in range(4):
                            top(ps[:, nb * 512:(nb + 1) * 512],
                                b_wT[:, kc * I3 + mc * 128:kc * I3 + (mc + 1) * 128],
                                b_xqT[:, kc * S + nb * 512:kc * S + (nb + 1) * 512],
                                start=(kc == 0), stop=(kc == DCH - 1))
                    stg = b_stage[:, (mc % 2) * S:(mc % 2) * S + S]
                    P.wait("scalar", "t")
                    P.wait("scalar", "v")
                    P.wait("scalar", "dg")
                    sop("activation", stg, ps, AF.Identity)
                    P.wait("vector", "s")
                    amx = b_amax[:, amax_i:amax_i + 1]
                    tmx = b_amax[:, 2 + amax_i:3 + amax_i]
                    vop("tensor_reduce", tmx, stg, AX, AL.max, apply_absolute_value=True)
                    if b == 0 and mc == 0:
                        vop("tensor_copy", amx, tmx)
                    else:
                        vop("tensor_tensor", amx, amx, tmx, AL.max)
                    P.wait("gpsimd", "v")
                    dma("gpsimd", "dg", rawdst[b][mc * 128:(mc + 1) * 128, :], stg)

    ph1()
    barrier_compute()
    cross_partition_scalar(b_amax[:, 0:1], SC_G2, b_amax[:, 1:2], SC_G3)
    mk_scale(SC_G2, SC_S2, SC_I2, SC_B2)
    mk_scale(SC_G3, SC_S3, SC_I3, SC_B3)
    vop("tensor_single_scalar", c_sc[:, SC_S3H:SC_S3H + 1],
        c_sc[:, SC_S3:SC_S3 + 1], 1.0 / 65536.0, AL.mult)
    vop("tensor_single_scalar", c_sc[:, SC_B3H:SC_B3H + 1],
        c_sc[:, SC_S3H:SC_S3H + 1], -32768.0, AL.mult)
    barrier_compute()

    cv = Carver()
    p_raw = cv.take(2 * S)
    p_z = cv.take(S)
    p_i32 = cv.take(S, I32)
    p_g = cv.take(S)
    p_cu = cv.take(S)
    p_cg = cv.take(S)
    p_cs = cv.take(S)
    p_sig = cv.take(S)
    p_gate2 = cv.take(S)
    p_p = cv.take(S)
    p_u16 = cv.take(S, U16)
    p_amax = cv.take(1)

    def ph3():
        for b in range(B_CORE):
            for mc in range(24):
                P.wait("sync", "v"); P.wait("sync", "s")
                dma("sync", "dl", p_raw[:, 0:S], h_upraw[b][mc * 128:(mc + 1) * 128, :])
                dma("sync", "dl", p_raw[:, S:2 * S],
                    h_gateraw[b][mc * 128:(mc + 1) * 128, :])
                P.wait("scalar", "dl"); P.wait("scalar", "v")
                sop("activation", p_z, p_raw[:, 0:S], AF.Identity, bias=c_sc[:, 23:24],
                    scale=c_sc[:, SC_I2:SC_I2 + 1])
                P.wait("vector", "s"); P.wait("vector", "dg")
                floor_fixup(p_z, p_i32, p_g, p_cu)
                vop("tensor_copy", p_u16, p_cu)
                P.wait("gpsimd", "v")
                dma("gpsimd", "dg", h_cup[b][mc * 128:(mc + 1) * 128, :], p_u16)
                P.wait("scalar", "v")
                sop("activation", p_z, p_raw[:, S:2 * S], AF.Identity, bias=c_sc[:, 23:24],
                    scale=c_sc[:, SC_I3:SC_I3 + 1])
                P.wait("vector", "s")
                floor_fixup(p_z, p_i32, p_g, p_cg)
                P.wait("scalar", "v")
                sop("activation", p_sig, p_cg, AF.Sigmoid,
                    bias=c_sc[:, SC_B3:SC_B3 + 1], scale=c_sc[:, SC_S3:SC_S3 + 1])
                sop("activation", p_gate2, p_cg, AF.Identity,
                    bias=c_sc[:, SC_B3H:SC_B3H + 1], scale=c_sc[:, SC_S3H:SC_S3H + 1])
                P.wait("vector", "s"); P.wait("vector", "dg")
                vop("tensor_copy", p_u16, p_cg)
                P.wait("gpsimd", "v")
                dma("gpsimd", "dg", h_cg[b][mc * 128:(mc + 1) * 128, :], p_u16)
                P.wait("scalar", "v")
                sop("activation", p_z, p_sig, AF.Identity, bias=c_sc[:, 24:25], scale=65536.0)
                P.wait("vector", "s"); P.wait("vector", "dg")
                floor_fixup(p_z, p_i32, p_g, p_cs)
                vop("tensor_single_scalar", p_cs, p_cs, 65535.0, AL.min)
                vop("tensor_copy", p_u16, p_cs)
                P.wait("gpsimd", "v")
                dma("gpsimd", "dg", h_cs[b][mc * 128:(mc + 1) * 128, :], p_u16)
                vop("tensor_tensor", p_p, p_cs, p_gate2, AL.mult)
                tmx = st_tmp[:, 28:29]
                vop("tensor_reduce", tmx, p_p, AX, AL.max, apply_absolute_value=True)
                if b == 0 and mc == 0:
                    vop("tensor_copy", p_amax, tmx)
                else:
                    vop("tensor_tensor", p_amax, p_amax, tmx, AL.max)

    ph3()
    barrier_compute()
    cross_partition_scalar(p_amax, SC_G4)
    mk_scale(SC_G4, SC_S4, SC_I4, SC_B4)
    barrier_compute()

    cv = Carver()
    q_a = cv.take(S, U16)
    q_b = cv.take(S, U16)
    q_c = cv.take(S, U16)
    q_gate2 = cv.take(S)
    q_p = cv.take(S)
    q_z = cv.take(S)
    q_i32 = cv.take(S, I32)
    q_g = cv.take(S)
    q_up = cv.take(S)
    q_u16 = cv.take(S, U16)
    q_amax = cv.take(1)

    def ph4():
        for b in range(B_CORE):
            for mc in range(24):
                P.wait("sync", "s"); P.wait("sync", "v")
                dma("sync", "dl", q_a, h_cg[b][mc * 128:(mc + 1) * 128, :])
                dma("sync", "dl", q_b, h_cs[b][mc * 128:(mc + 1) * 128, :])
                dma("sync", "dl", q_c, h_cup[b][mc * 128:(mc + 1) * 128, :])
                P.wait("scalar", "dl"); P.wait("scalar", "v")
                sop("activation", q_gate2, q_a, AF.Identity,
                    bias=c_sc[:, SC_B3H:SC_B3H + 1], scale=c_sc[:, SC_S3H:SC_S3H + 1])
                P.wait("vector", "s")
                vop("scalar_tensor_tensor", q_p, q_b, 65535.0, q_gate2, AL.min, AL.mult)
                P.wait("scalar", "v")
                sop("activation", q_z, q_p, AF.Identity, bias=c_sc[:, 23:24],
                    scale=c_sc[:, SC_I4:SC_I4 + 1])
                P.wait("vector", "s"); P.wait("vector", "dg")
                floor_fixup(q_z, q_i32, q_g, q_p)
                vop("tensor_copy", q_u16, q_p)
                P.wait("gpsimd", "v")
                dma("gpsimd", "dg", h_cg2[b][mc * 128:(mc + 1) * 128, :], q_u16)
                P.wait("scalar", "v")
                sop("activation", q_up, q_c, AF.Identity,
                    bias=c_sc[:, SC_B2:SC_B2 + 1], scale=c_sc[:, SC_S2:SC_S2 + 1])
                P.wait("vector", "s")
                vop("scalar_tensor_tensor", q_z, q_p, -32768.0, q_up, AL.add, AL.mult)
                tmx = st_tmp[:, 28:29]
                vop("tensor_reduce", tmx, q_z, AX, AL.max, apply_absolute_value=True)
                vop("tensor_tensor", tmx, tmx, c_sc[:, SC_S4:SC_S4 + 1], AL.mult)
                if b == 0 and mc == 0:
                    vop("tensor_copy", q_amax, tmx)
                else:
                    vop("tensor_tensor", q_amax, q_amax, tmx, AL.max)

    ph4()
    barrier_compute()
    cross_partition_scalar(q_amax, SC_G5)
    mk_scale(SC_G5, SC_S5, SC_I5, SC_B5)
    vop("tensor_tensor", c_sc[:, SC_I5H:SC_I5H + 1],
        c_sc[:, SC_I5:SC_I5 + 1], c_sc[:, SC_S4:SC_S4 + 1], AL.mult)
    barrier_compute()

    cv = Carver()
    r_a = cv.take(S, U16)
    r_c = cv.take(S, U16)
    r_up = cv.take(S)
    r_q = cv.take(S)
    r_z = cv.take(S)
    r_i32 = cv.take(S, I32)
    r_g = cv.take(S)
    r_obf = cv.take(S, BF16)
    r_wd = cv.take(24 * DM, BF16)
    r_ocol = cv.take(2 * 24 * 128, BF16)
    r_row = cv.take(DM)
    r_sq = cv.take(DM)
    r_x2 = cv.take(DM)

    def ph5():
        P.wait("sync", "t")
        dma("sync", "dl", r_wd.rearrange("p (kc m) -> p kc m", kc=hw_downT.shape[0] // 128),
            hw_downT.rearrange("(kc p) m -> p kc m", p=128))
        for b in range(B_CORE):
            for kc in range(24):
                P.wait("sync", "s"); P.wait("sync", "v")
                dma("sync", "dl", r_a, h_cg2[b][kc * 128:(kc + 1) * 128, :])
                dma("sync", "dl", r_c, h_cup[b][kc * 128:(kc + 1) * 128, :])
                P.wait("scalar", "dl"); P.wait("scalar", "v")
                sop("activation", r_up, r_c, AF.Identity,
                    bias=c_sc[:, SC_B2:SC_B2 + 1], scale=c_sc[:, SC_S2:SC_S2 + 1])
                P.wait("vector", "s")
                vop("scalar_tensor_tensor", r_q, r_a, -32768.0, r_up, AL.add, AL.mult)
                P.wait("scalar", "v")
                sop("activation", r_z, r_q, AF.Identity, bias=c_sc[:, 23:24],
                    scale=c_sc[:, SC_I5H:SC_I5H + 1])
                P.wait("vector", "s")
                floor_fixup(r_z, r_i32, r_g, r_z)
                P.wait("scalar", "v"); P.wait("scalar", "dg")
                sop("activation", r_obf, r_z, AF.Identity,
                    bias=c_sc[:, SC_B5:SC_B5 + 1], scale=c_sc[:, SC_S5:SC_S5 + 1])
                P.wait("gpsimd", "s")
                dma("gpsimd", "dg", h_obf[b][kc * 128:(kc + 1) * 128, :], r_obf)
            P.wait("sync", "dg")
            for mc in range(NST):
                par = mc % 2
                ocol = r_ocol[:, par * 24 * 128:(par + 1) * 24 * 128]
                P.wait("sync", "t")
                dma("sync", "dl", ocol.rearrange("p (kc m) -> p kc m", m=128),
                    h_obf[b][:, mc * 128:(mc + 1) * 128].rearrange(
                        "(kc p) m -> p kc m", p=128))
                r0 = b * S + mc * 128
                P.wait("sync", "v")
                dma("sync", "dl", r_row, h_x1[r0:r0 + 128, :])
                ps = psA if par == 0 else psB
                P.wait("tensor", "dl"); P.wait("tensor", "v")
                for kc in range(24):
                    for nb in range(2):
                        top(ps[:, nb * 512:(nb + 1) * 512],
                            ocol[:, kc * 128:(kc + 1) * 128],
                            r_wd[:, kc * DM + nb * 512:kc * DM + (nb + 1) * 512],
                            start=(kc == 0), stop=(kc == 23))
                P.wait("vector", "t"); P.wait("vector", "dl")
                vop("tensor_copy", r_sq, ps[:, 0:1024])
                mcol = st_tmp[:, 16:17]; rcol = st_tmp[:, 17:18]
                rowstats(r_sq, DM, mcol, rcol, 1e-5, r_z[:, 0:DM])
                bias = st_tmp[:, 18:19]
                negmr(mcol, rcol, bias)
                P.wait("scalar", "v")
                sop("activation", r_z[:, 0:DM], r_sq, AF.Identity, bias=bias, scale=rcol)
                P.wait("vector", "s")
                vop("tensor_tensor", r_x2, r_row, r_z[:, 0:DM], AL.add)
                m5 = st_m["ln5"][:, b * 16 + mc:b * 16 + mc + 1]
                r5 = st_r["ln5"][:, b * 16 + mc:b * 16 + mc + 1]
                rowstats(r_x2, DM, m5, r5, 1e-5, r_z[:, 0:DM])
                P.wait("sync", "v")
                dma("sync", "dl", h_x2[r0:r0 + 128, :], r_x2)

    ph5()
    barrier_compute()
    if DEV_STOP[0] == "x2":
        dev_copyout(h_x2, r_row)
        return finish(nc, P, sem_cms, ctxs)

    cvF = Carver()
    m_row = cvF.take(DM)
    m_u0 = cvF.take(DM)
    m_sq = cvF.take(DM)
    m_u1 = cvF.take(DM)
    m_u1bf = cvF.take(DM, BF16)
    m_u1T = cvF.take(DCH * S, BF16)
    m_win = cvF.take(DCH * 4096, BF16)
    m_xh = cvF.take(S + 8)
    m_acc = cvF.take(S)
    m_xc = cvF.take(S)
    m_xcbf = cvF.take(S, BF16)
    m_wx = cvF.take(16 * 96, BF16)

    def ph6_front(b):
        rows0 = b * S
        for st in range(NST):
            r0 = rows0 + st * 128
            P.wait("sync", "s"); P.wait("sync", "v")
            dma("sync", "dl", m_row, h_x2[r0:r0 + 128, :])
            m5 = st_m["ln5"][:, b * 16 + st:b * 16 + st + 1]
            r5 = st_r["ln5"][:, b * 16 + st:b * 16 + st + 1]
            bias = st_tmp[:, 8:9]
            P.wait("vector", "dl"); P.wait("vector", "s")
            negmr(m5, r5, bias)
            P.wait("scalar", "v")
            sop("activation", m_u0, m_row, AF.Identity, bias=bias, scale=r5)
            P.wait("vector", "s")
            m6 = st_tmp[:, 9:10]; r6 = st_tmp[:, 10:11]
            rowstats(m_u0, DM, m6, r6, 1e-5, m_sq)
            bias2 = st_tmp[:, 11:12]
            negmr(m6, r6, bias2)
            P.wait("scalar", "v")
            sop("activation", m_sq, m_u0, AF.Identity, bias=bias2, scale=r6)
            P.wait("vector", "s")
            vop("tensor_tensor", m_u1, m_sq, c_lng, AL.mult)
            vop("tensor_tensor", m_u1, m_u1, c_lnb, AL.add)
            P.wait("scalar", "v")
            sop("activation", m_u1bf, m_u1, AF.Identity)
            P.wait("sync", "v"); P.wait("sync", "s")
            dma("sync", "dl", h_u1[r0:r0 + 128, :], m_u0)
            dma("sync", "dl", h_tbf[r0:r0 + 128, :], m_u1bf)
        P.wait("sync", "t")
        for kc in range(DCH):
            dma("sync", "dl", m_u1T[:, kc * S:(kc + 1) * S],
                h_tbf[rows0:rows0 + S, kc * 128:(kc + 1) * 128], transpose=True)
        dma("sync", "dl", m_win.rearrange("p (kc m) -> p kc m", kc=hw_in.shape[0] // 128),
            hw_in.rearrange("(kc p) m -> p kc m", p=128))
        dma("sync", "dl", m_wx.rearrange("p (g m) -> p g m", m=96),
        hw_x.rearrange("(g p) m -> p g m", p=128))
        P.wait("tensor", "dl")
        for dt_i in range(16):
            P.wait("tensor", "s")
            for kc in range(DCH):
                for nb in range(4):
                    top(psA[:, nb * 512:(nb + 1) * 512],
                        m_win[:, kc * 4096 + dt_i * 128:kc * 4096 + (dt_i + 1) * 128],
                        m_u1T[:, kc * S + nb * 512:kc * S + (nb + 1) * 512],
                        start=(kc == 0), stop=(kc == DCH - 1))
            P.wait("vector", "s")
            vop("memset", m_xh[:, 0:3], 0.0)
            P.wait("scalar", "t"); P.wait("scalar", "v")
            sop("activation", m_xh[:, 3:3 + S], psA, AF.Identity)
            sop("activation", m_acc, m_xh[:, 0:S], AF.Identity,
                bias=c_cb[:, dt_i:dt_i + 1], scale=c_cw[:, dt_i * 4:dt_i * 4 + 1])
            P.wait("vector", "s")
            for k in range(1, 4):
                vop("scalar_tensor_tensor", m_acc, m_xh[:, k:k + S],
                    c_cw[:, dt_i * 4 + k:dt_i * 4 + k + 1], m_acc, AL.mult, AL.add)
            P.wait("scalar", "v"); P.wait("scalar", "t")
            sop("activation", m_xc, m_acc, AF.Silu)
            sop("activation", m_xcbf, m_acc, AF.Silu)
            P.wait("sync", "s")
            dma("sync", "dl", h_xc[b][dt_i * 128:(dt_i + 1) * 128, :], m_xc)
            P.wait("tensor", "s")
            for nb in range(4):
                top(psB[0:96, nb * 512:(nb + 1) * 512],
                    m_wx[:, dt_i * 96:(dt_i + 1) * 96],
                    m_xcbf[:, nb * 512:(nb + 1) * 512],
                    start=(dt_i == 0), stop=(dt_i == 15))
        for dt_i in range(16):
            P.wait("tensor", "s")
            for kc in range(DCH):
                for nb in range(4):
                    top(psA[:, nb * 512:(nb + 1) * 512],
                        m_win[:, kc * 4096 + 2048 + dt_i * 128:
                              kc * 4096 + 2048 + (dt_i + 1) * 128],
                        m_u1T[:, kc * S + nb * 512:kc * S + (nb + 1) * 512],
                        start=(kc == 0), stop=(kc == DCH - 1))
            P.wait("scalar", "t"); P.wait("scalar", "dg")
            sop("activation", m_xc, psA, AF.Silu)
            P.wait("gpsimd", "s")
            dma("gpsimd", "dg", h_zs[b][dt_i * 128:(dt_i + 1) * 128, :], m_xc)
        P.wait("scalar", "t")
        sop("activation", m_dbc, psB[0:96, 0:S], AF.Identity)
        P.wait("scalar", "s")
        sop("activation", m_dtpre, m_dbc[0:64, :], AF.Identity)
        P.wait("sync", "s")
        dma("sync", "dl", h_bc[b], m_dbc[64:96, :])
        P.wait("sync", "dl")

    cvS = Carver()
    s_dA = cvS.take(NSTATE * TC)
    s_dBu = cvS.take(NSTATE * TC)
    s_Brep = cvS.take(NSTATE * TC)
    s_Crep = cvS.take(NSTATE * TC)
    s_dt = cvS.take(TC)
    s_sp1 = cvS.take(TC)
    s_sp2 = cvS.take(TC)
    s_xc = cvS.take(TC)
    s_du = cvS.take(TC)
    s_z = cvS.take(TC)
    s_y = cvS.take(TC)
    s_ygbf = cvS.take(TC, BF16)
    s_hcar = cvS.take(16 * NSTATE)
    s_tmpc = cvS.take(NSTATE)

    def ph6_scan(b):
        dA3 = s_dA.rearrange("p (n t) -> p n t", n=NSTATE)
        dBu3 = s_dBu.rearrange("p (n t) -> p n t", n=NSTATE)
        for ch in range(NCH):
            t0 = ch * TC
            P.wait("sync", "v")
            for n in range(NSTATE):
                dma("sync", "dl", s_Brep[:, n * TC:(n + 1) * TC],
                    h_bc[b][n:n + 1, t0:t0 + TC].broadcast_to([128, TC]))
                dma("sync", "dl", s_Crep[:, n * TC:(n + 1) * TC],
                    h_bc[b][16 + n:17 + n, t0:t0 + TC].broadcast_to([128, TC]))
            for dt_i in range(16):
                P.wait("tensor", "s")
                for nb in range(TC // 512):
                    top(psB[:, nb * 512:(nb + 1) * 512],
                        s_wdt[0:64, dt_i * 128:(dt_i + 1) * 128],
                        m_dtpre[0:64, t0 + nb * 512:t0 + (nb + 1) * 512],
                        start=True, stop=True)
                P.wait("scalar", "t"); P.wait("scalar", "v")
                sop("activation", s_sp1, psB[:, 0:TC], AF.Abs,
                    bias=c_bdt[:, dt_i:dt_i + 1])
                sop("activation", s_sp2, psB[:, 0:TC], AF.Identity,
                    bias=c_bdt[:, dt_i:dt_i + 1])
                sop("activation", s_sp1, s_sp1, AF.Exp, scale=-1.0)
                sop("activation", s_sp1, s_sp1, AF.Ln, bias=1.0)
                P.wait("vector", "s")
                vop("scalar_tensor_tensor", s_dt, s_sp2, 0.0, s_sp1, AL.max, AL.add)
                P.wait("sync", "v")
                dma("sync", "dl", s_xc,
                    h_xc[b][dt_i * 128:(dt_i + 1) * 128, t0:t0 + TC])
                P.wait("vector", "dl")
                vop("tensor_tensor", s_du, s_dt, s_xc, AL.mult)
                P.wait("scalar", "v")
                for n in range(NSTATE):
                    sop("activation", s_dA[:, n * TC:(n + 1) * TC], s_dt, AF.Exp,
                        scale=c_A[:, dt_i * NSTATE + n:dt_i * NSTATE + n + 1])
                P.wait("vector", "s"); P.wait("vector", "dl")
                vop("tensor_tensor", dBu3,
                    s_du.unsqueeze(1).broadcast_to([128, NSTATE, TC]),
                    s_Brep.rearrange("p (n t) -> p n t", n=NSTATE), AL.mult)
                car = s_hcar[:, dt_i * NSTATE:(dt_i + 1) * NSTATE]
                if ch == 0:
                    vop("memset", car, 0.0)
                car3 = car.unsqueeze(2)
                vop("tensor_tensor", s_tmpc.unsqueeze(2), dA3[:, :, 0:1], car3, AL.mult)
                vop("tensor_tensor", dBu3[:, :, 0:1], dBu3[:, :, 0:1],
                    s_tmpc.unsqueeze(2), AL.add)
                vop("tensor_single_scalar", dA3[:, :, 0:1], dA3[:, :, 0:1],
                    0.0, AL.mult)
                vop("tensor_tensor_scan", s_dA, s_dA, s_dBu, 0.0, AL.mult, AL.add)
                vop("tensor_copy", car3, dA3[:, :, TC - 1:TC])
                if DEV_STOP[0] == "scan1" and b == 0 and ch == 1 and dt_i == 0:
                    P.wait("sync", "v")
                    dma("sync", "dl", out_h[0:128, :], s_dA[:, 0:1024])
                    dma("sync", "dl", out_h[128:256, :], s_dBu[:, 0:1024])
                    dma("sync", "dl", out_h[256:384, 0:256], s_hcar)
                    dma("sync", "dl", out_h[384:512, 0:512], s_dt)
                    dma("sync", "dl", out_h[512:640, 0:512], s_xc)
                    P.wait("sync", "dl")
                    P.wait("vector", "dl")
                vop("tensor_tensor", s_dA, s_dA, s_Crep, AL.mult)
                if DEV_STOP[0] == "scan1" and b == 0 and ch == 1 and dt_i == 0:
                    P.wait("sync", "v")
                    dma("sync", "dl", out_h[640:768, :], s_dA[:, 0:1024])
                    dma("sync", "dl", out_h[768:896, :], s_dA[:, 8 * TC:8 * TC + 1024])
                    P.wait("sync", "dl")
                    P.wait("vector", "dl")
                vop("tensor_tensor", s_dBu[:, 0:8 * TC], s_dA[:, 0:8 * TC],
                    s_dA[:, 8 * TC:16 * TC], AL.add)
                vop("tensor_tensor", s_dA[:, 0:4 * TC], s_dBu[:, 0:4 * TC],
                    s_dBu[:, 4 * TC:8 * TC], AL.add)
                vop("tensor_tensor", s_dBu[:, 0:2 * TC], s_dA[:, 0:2 * TC],
                    s_dA[:, 2 * TC:4 * TC], AL.add)
                vop("tensor_tensor", s_y, s_dBu[:, 0:TC], s_dBu[:, TC:2 * TC], AL.add)
                if DEV_STOP[0] == "scan1" and b == 0 and ch == 1 and dt_i == 0:
                    P.wait("sync", "v")
                    dma("sync", "dl", out_h[896:1024, 0:512], s_y)
                    P.wait("sync", "dl")
                    P.wait("vector", "dl")
                vop("scalar_tensor_tensor", s_y, s_xc,
                    c_D[:, dt_i:dt_i + 1], s_y, AL.mult, AL.add)
                P.wait("sync", "v")
                dma("sync", "dl", s_z,
                    h_zs[b][dt_i * 128:(dt_i + 1) * 128, t0:t0 + TC])
                P.wait("vector", "dl")
                vop("tensor_tensor", s_y, s_y, s_z, AL.mult)
                P.wait("scalar", "v"); P.wait("scalar", "dg")
                sop("activation", s_ygbf, s_y, AF.Identity)
                P.wait("gpsimd", "s")
                dma("gpsimd", "dg",
                    h_yg[b][dt_i * 128:(dt_i + 1) * 128, t0:t0 + TC], s_ygbf)

    cvE = Carver()
    e_yg = cvE.take(2 * 16 * 128, BF16)
    e_wo = cvE.take(16 * DM, BF16)
    e_u1 = cvE.take(DM)
    e_row = cvE.take(DM)
    e_sq = cvE.take(DM)
    e_x3 = cvE.take(DM)

    def ph6_out(b):
        P.wait("sync", "t")
        dma("sync", "dl", e_wo.rearrange("p (kc m) -> p kc m", kc=hw_out.shape[0] // 128),
            hw_out.rearrange("(kc p) m -> p kc m", p=128))
        for mc in range(NST):
            par = mc % 2
            yg = e_yg[:, par * 16 * 128:(par + 1) * 16 * 128]
            r0 = b * S + mc * 128
            P.wait("sync", "t")
            dma("sync", "dl", yg.rearrange("p (kc m) -> p kc m", m=128),
                h_yg[b][:, mc * 128:(mc + 1) * 128].rearrange(
                    "(kc p) m -> p kc m", p=128))
            P.wait("sync", "v")
            dma("sync", "dl", e_u1, h_u1[r0:r0 + 128, :])
            dma("sync", "dl", e_row, h_x2[r0:r0 + 128, :])
            ps = psA if par == 0 else psB
            P.wait("tensor", "dl"); P.wait("tensor", "v")
            for kc in range(16):
                for nb in range(2):
                    top(ps[:, nb * 512:(nb + 1) * 512],
                        yg[:, kc * 128:(kc + 1) * 128],
                        e_wo[:, kc * DM + nb * 512:kc * DM + (nb + 1) * 512],
                        start=(kc == 0), stop=(kc == 15))
            P.wait("vector", "t"); P.wait("vector", "dl")
            vop("tensor_tensor", e_x3, ps[:, 0:1024], e_u1, AL.add)
            mcol = st_tmp[:, 16:17]; rcol = st_tmp[:, 17:18]
            rowstats(e_x3, DM, mcol, rcol, 1e-5, e_sq)
            bias = st_tmp[:, 18:19]
            negmr(mcol, rcol, bias)
            P.wait("scalar", "v")
            sop("activation", e_sq, e_x3, AF.Identity, bias=bias, scale=rcol)
            P.wait("vector", "s")
            vop("tensor_tensor", e_x3, e_row, e_sq, AL.add)
            m7 = st_m["ln6"][:, b * 16 + mc:b * 16 + mc + 1]
            r7 = st_r["ln6"][:, b * 16 + mc:b * 16 + mc + 1]
            rowstats(e_x3, DM, m7, r7, 1e-5, e_sq)
            P.wait("sync", "v")
            dma("sync", "dl", h_x3[r0:r0 + 128, :], e_x3)

    for b in range(B_CORE):
        ph6_front(b)
        barrier_compute()
        if b == 0 and DEV_STOP[0] in ("u1", "xc", "zs", "bc", "dtp"):
            if DEV_STOP[0] == "u1":
                for st in range(NST):
                    r0 = st * 128
                    P.wait("sync", "dl")
                    dma("sync", "dl", m_row, h_u1[r0:r0 + 128, :])
                    P.wait("sync", "dl")
                    dma("sync", "dl", out_h[r0:r0 + 128, :], m_row)
            elif DEV_STOP[0] == "xc":
                dma("gpsimd", "dg", out_h[0:2048, :], h_xc[0][:, 0:1024])
            elif DEV_STOP[0] == "zs":
                dma("gpsimd", "dg", out_h[0:2048, :], h_zs[0][:, 0:1024])
            elif DEV_STOP[0] == "bc":
                dma("gpsimd", "dg", out_h[0:32, :], h_bc[0][:, 0:1024])
            elif DEV_STOP[0] == "dtp":
                sop("activation", m_xc[0:64, 0:1024], m_dtpre[0:64, 0:1024], AF.Identity)
                P.wait("sync", "s")
                dma("sync", "dl", out_h[0:64, :], m_xc[0:64, 0:1024])
            P.wait("sync", "dl"); P.wait("sync", "dg")
            return finish(nc, P, sem_cms, ctxs)
        ph6_scan(b)
        barrier_compute()
        if b == 0 and DEV_STOP[0] == "scan1":
            P.wait("sync", "dl"); P.wait("sync", "dg")
            return finish(nc, P, sem_cms, ctxs)
        if b == 0 and DEV_STOP[0] == "yg":
            dma("gpsimd", "dg", out_h[0:2048, :], h_yg[0][:, 0:1024])
            P.wait("sync", "dl"); P.wait("sync", "dg")
            return finish(nc, P, sem_cms, ctxs)
        ph6_out(b)
        barrier_compute()

    if DEV_STOP[0] == "x3":
        dev_copyout(h_x3, m_row)
        return finish(nc, P, sem_cms, ctxs)

    cv = Carver()
    f_row = cv.take(DM)
    f_xn = cv.take(DM)
    f_bmax = cv.take(32)
    f_inv = cv.take(32)
    f_s = cv.take(32)
    f_q = cv.take(DM)
    f_t1 = cv.take(DM)
    f_t2 = cv.take(DM)
    f_i32 = cv.take(DM, I32)
    f_g = cv.take(DM)
    f_bf = cv.take(DM, BF16)
    f_xdT = cv.take(DCH * S, BF16)
    f_wT = cv.take(DCH * DM, BF16)
    f_sq = cv.take(DM)
    f_x4 = cv.take(DM)

    def ph7():
        P.wait("sync", "t")
        dma("sync", "dl", f_wT.rearrange("p (kc m) -> p kc m", kc=hw_fT.shape[0] // 128),
            hw_fT.rearrange("(kc p) m -> p kc m", p=128))
        for b in range(B_CORE):
            rows0 = b * S
            for st in range(NST):
                r0 = rows0 + st * 128
                P.wait("sync", "s"); P.wait("sync", "v")
                dma("sync", "dl", f_row, h_x3[r0:r0 + 128, :])
                m7 = st_m["ln6"][:, b * 16 + st:b * 16 + st + 1]
                r7 = st_r["ln6"][:, b * 16 + st:b * 16 + st + 1]
                bias = st_tmp[:, 8:9]
                P.wait("vector", "dl"); P.wait("vector", "s")
                negmr(m7, r7, bias)
                P.wait("scalar", "v")
                sop("activation", f_xn, f_row, AF.Identity, bias=bias, scale=r7)
                P.wait("vector", "s")
                x3r = f_xn.rearrange("p (g k) -> p g k", k=32)
                vop("tensor_reduce", f_bmax, x3r, AX, AL.max, apply_absolute_value=True)
                vop("tensor_single_scalar", f_s, f_bmax, 1.0 / 7.5, AL.mult)
                vop("tensor_single_scalar", f_s, f_s, 1e-30, AL.max)
                vop("reciprocal", f_inv, f_s)
                q3 = f_q.rearrange("p (g k) -> p g k", k=32)
                vop("tensor_tensor", q3, x3r,
                    f_inv.unsqueeze(2).broadcast_to([128, 32, 32]), AL.mult)
                P.wait("scalar", "v")
                sop("activation", f_t2, f_q, AF.Abs)
                P.wait("vector", "s")
                vop("tensor_single_scalar", f_t1, f_t2, 2.0, AL.is_ge)
                vop("tensor_single_scalar", f_t1, f_t1, -4.0, AL.mult)
                vop("tensor_single_scalar", f_t1, f_t1, 8.0, AL.add)
                vop("tensor_single_scalar", f_t2, f_t2, 4.0, AL.is_ge)
                vop("tensor_single_scalar", f_t2, f_t2, -2.0, AL.mult)
                vop("tensor_tensor", f_t1, f_t1, f_t2, AL.add)
                vop("tensor_tensor", f_q, f_q, f_t1, AL.mult)
                vop("tensor_single_scalar", f_q, f_q, -1.0, AL.mult)
                vop("tensor_single_scalar", f_q, f_q, 32768.5, AL.add)
                floor_fixup(f_q, f_i32, f_g, f_q)
                vop("tensor_single_scalar", f_q, f_q, -32768.0, AL.add)
                vop("tensor_single_scalar", f_q, f_q, -1.0, AL.mult)
                vop("reciprocal", f_t1, f_t1)
                vop("tensor_tensor", f_q, f_q, f_t1, AL.mult)
                b3 = f_bf.rearrange("p (g k) -> p g k", k=32)
                vop("tensor_tensor", b3, f_q.rearrange("p (g k) -> p g k", k=32),
                    f_s.unsqueeze(2).broadcast_to([128, 32, 32]), AL.mult)
                P.wait("sync", "v")
                dma("sync", "dl", h_tbf[r0:r0 + 128, :], f_bf)
            P.wait("sync", "t")
            for kc in range(DCH):
                dma("sync", "dl", f_xdT[:, kc * S:(kc + 1) * S],
                    h_tbf[rows0:rows0 + S, kc * 128:(kc + 1) * 128], transpose=True)
            P.wait("tensor", "dl")
            for mc in range(NST):
                par = mc % 2
                ps = psA if par == 0 else psB
                P.wait("tensor", "s"); P.wait("tensor", "v")
                for kc in range(DCH):
                    for nb in range(2):
                        top(ps[:, nb * 512:(nb + 1) * 512],
                            f_xdT[:, kc * S + mc * 128:kc * S + (mc + 1) * 128],
                            f_wT[:, kc * DM + nb * 512:kc * DM + (nb + 1) * 512],
                            start=(kc == 0), stop=(kc == DCH - 1))
                r0 = rows0 + mc * 128
                P.wait("sync", "v")
                dma("sync", "dl", f_row, h_x3[r0:r0 + 128, :])
                P.wait("vector", "t"); P.wait("vector", "dl")
                vop("tensor_tensor", f_x4, ps[:, 0:1024], c_fb, AL.add)
                mcol = st_tmp[:, 16:17]; rcol = st_tmp[:, 17:18]
                rowstats(f_x4, DM, mcol, rcol, 1e-5, f_sq)
                bias = st_tmp[:, 18:19]
                negmr(mcol, rcol, bias)
                P.wait("scalar", "v")
                sop("activation", f_sq, f_x4, AF.Identity, bias=bias, scale=rcol)
                P.wait("vector", "s")
                vop("tensor_tensor", f_x4, f_row, f_sq, AL.add)
                P.wait("sync", "v")
                dma("sync", "dl", out_h[r0:r0 + 128, :], f_x4)

    ph7()
    P.wait("sync", "dl")
    P.wait("sync", "dg")
    return finish(nc, P, sem_cms, ctxs)


def finish(nc, P, sem_cms, ctxs):
    blk_cm = nc.Block()
    block = blk_cm.__enter__()
    SM = sem_cms

    def runner(eng_name):
        def body(E):
            for fn in P.prog[eng_name]:
                try:
                    fn(E, SM)
                except Exception:
                    import sys
                    tb = getattr(fn, "_plan_site", None)
                    print("EMIT FAILURE at plan site:", tb, file=sys.stderr)
                    raise
        return body

    block.sync(runner("sync"))
    block.gpsimd(runner("gpsimd"))
    block.vector(runner("vector"))
    block.scalar(runner("scalar"))
    block.tensor(runner("tensor"))
    blk_cm.__exit__(None, None, None)
    for cm in reversed(ctxs):
        cm.__exit__(None, None, None)
    return nc
